# version 01
# baseline (speedup 1.0000x reference)
import contextlib
import numpy as np
import ml_dtypes
import concourse.bass as bass
import concourse.mybir as mybir
from concourse.bass_utils import run_bass_kernel_spmd

F32 = mybir.dt.float32
BF16 = mybir.dt.bfloat16
ALU = mybir.AluOpType
ACT = mybir.ActivationFunctionType
AX = mybir.AxisListType
BB = ml_dtypes.bfloat16

NCORES = 8
BSZ = 2
L = 4096
D = 2048
N = 16
H = 8192
E = 2080
TA = 1024
DS = D // NCORES
HS = H // NCORES
KD = D // 128
KH = H // 128
NT_A = TA // 128
TB = 512
NTB = L // TB

LADDER = [64.0, 32.0, 16.0, 8.0, 4.0, 2.0, 1.0]
CONSTS = [1e-5, 1e-8, -0.5, 1.0 / 127.0, 0.5, 2.0, 1.0 / D]
EPS1_I, EPS2_I, NEGH_I, INV127_I, HALF_I, TWO_I, INVD_I = range(7)


class Ctr:

    def __init__(self):
        self.v = {}

    def mark(self, key, val):
        self.v[key] = val

    def get(self, key):
        return self.v.get(key, 0)


def _mk(nc, es, name, shape, dt=F32, psum=False):
    f = nc.psum_tensor if psum else nc.sbuf_tensor
    return es.enter_context(f(name, shape, dt))


class DmaSlots:

    def __init__(self, nc, es, name, nslots):
        self.sems = [es.enter_context(nc.semaphore(f"{name}{i}"))
                     for i in range(nslots)]
        self.count = [0] * nslots

    def issue(self, eng, slot, out, in_):
        eng.dma_start(out=out, in_=in_).then_inc(self.sems[slot], 16)
        self.count[slot] += 16
        return self.count[slot]

    def wait(self, eng, slot, val):
        eng.wait_ge(self.sems[slot], val)


def _consts_setup(nc, es):
    cst_h = nc.dram_tensor("cst", [128, len(CONSTS)], F32, kind="ExternalInput").ap()
    cst = _mk(nc, es, "cstp", [128, len(CONSTS)])
    for i, val in enumerate(CONSTS):
        nc.const_aps.register(F32, float(val), cst[:, i:i + 1])
    return cst_h, cst


def _host_inputs_consts():
    return np.tile(np.asarray(CONSTS, np.float32), (128, 1))



def build_phase_a(ctr):
    nc = bass.Bass("TRN2", target_bir_lowering=False)
    xa = nc.dram_tensor("xa", [TA, D], F32, kind="ExternalInput").ap()
    wxT = nc.dram_tensor("wxT", [D, E], F32, kind="ExternalInput").ap()
    wg_sl = nc.dram_tensor("wg_sl", [DS, H], F32, kind="ExternalInput").ap()
    wu_sl = nc.dram_tensor("wu_sl", [DS, H], F32, kind="ExternalInput").ap()
    wd_sl = nc.dram_tensor("wd_sl", [HS, D], F32, kind="ExternalInput").ap()
    eye_h = nc.dram_tensor("eye", [128, 128], BF16, kind="ExternalInput").ap()

    dt_o = nc.dram_tensor("dt_o", [D, TA], F32, kind="ExternalOutput").ap()
    xnT_o = nc.dram_tensor("xnT_o", [D, TA], BF16, kind="ExternalOutput").ap()
    bc_o = nc.dram_tensor("bc_o", [32, TA], BF16, kind="ExternalOutput").ap()
    sg_o = nc.dram_tensor("sg_o", [DS, H], BF16, kind="ExternalOutput").ap()
    su_o = nc.dram_tensor("su_o", [DS, H], BF16, kind="ExternalOutput").ap()
    sd_o = nc.dram_tensor("sd_o", [HS, D], BF16, kind="ExternalOutput").ap()

    with contextlib.ExitStack() as es:
        cst_h, cst = _consts_setup(nc, es)
        eye = _mk(nc, es, "eye_s", [128, 128], BF16)
        x_t = _mk(nc, es, "x_t", [128, 2 * D])
        sq_t = _mk(nc, es, "sq_t", [128, 2 * D], BF16)
        st_c = _mk(nc, es, "st_c", [128, 8])
        xn_t = _mk(nc, es, "xn_t", [128, 2 * D], BF16)
        wx_st = _mk(nc, es, "wx_st", [128, 2 * E])
        wx_b = _mk(nc, es, "wx_b", [128, KD * E], BF16)
        xnT = _mk(nc, es, "xnT_s", [128, KD * TA], BF16)
        ev_t = _mk(nc, es, "ev_t", [128, 4 * 512])
        sp_t = _mk(nc, es, "sp_t", [128, 2 * 512])
        bc_bf = _mk(nc, es, "bc_bf", [32, 2 * 512], BF16)
        w_sl = _mk(nc, es, "w_sl", [128, H])
        s_sl = _mk(nc, es, "s_sl", [128, H], BF16)
        tp_ps = _mk(nc, es, "tp_ps", [128, 2 * 128], BF16, psum=True)
        mm_ps = _mk(nc, es, "mm_ps", [128, 2 * 512], psum=True)

        lc = es.enter_context(nc.semaphore("lc"))
        lx = DmaSlots(nc, es, "lx", 2)
        lwx = DmaSlots(nc, es, "lwx", 2)
        lw = DmaSlots(nc, es, "lw", 2)
        cv = es.enter_context(nc.semaphore("cv"))
        ca = es.enter_context(nc.semaphore("ca"))
        ct = es.enter_context(nc.semaphore("ct"))
        soev = DmaSlots(nc, es, "soev", 4)
        sosg = DmaSlots(nc, es, "sosg", 2)
        sox = es.enter_context(nc.semaphore("sox"))
        block = es.enter_context(nc.Block())

        WSL = []
        for srcw in (wg_sl, wu_sl):
            for row in range(2):
                for half in range(2):
                    WSL.append((srcw, row, half * (H // 2), H // 2))
        for row in range(8):
            WSL.append((wd_sl, row, 0, D))

        @block.sync
        def _(sync):
            sync.dma_start(out=cst[:], in_=cst_h[:]).then_inc(lc, 16)
            sync.dma_start(out=eye[:], in_=eye_h[:]).then_inc(lc, 16)
            for it in range(NT_A):
                if it >= 2:
                    sync.wait_ge(cv, ctr.get(("a_xn", it - 2)))
                    sync.wait_ge(ca, ctr.get(("a_sq", it - 2)))
                ctr.mark(("a_ldx", it),
                         lx.issue(sync, it % 2,
                                  x_t[:, (it % 2) * D:(it % 2 + 1) * D],
                                  xa[it * 128:(it + 1) * 128, :]))
            for k in range(KD):
                if k >= 2:
                    sync.wait_ge(cv, ctr.get(("a_wxc", k - 2)))
                ctr.mark(("a_ldw", k),
                         lwx.issue(sync, k % 2,
                                   wx_st[:, (k % 2) * E:(k % 2 + 1) * E],
                                   wxT[k * 128:(k + 1) * 128, :]))
            for i, (src_t, row, c0, wid) in enumerate(WSL):
                if i >= 2:
                    sync.wait_ge(ca, ctr.get(("a_sign", i - 2)))
                ctr.mark(("a_ldwsl", i),
                         lw.issue(sync, i % 2,
                                  w_sl[:, (i % 2) * (H // 2):
                                       (i % 2) * (H // 2) + wid],
                                  src_t[row * 128:(row + 1) * 128, c0:c0 + wid]))

        @block.vector
        def _(vector):
            c = [0]

            def inc(instr, n=1):
                c[0] += n
                instr.then_inc(cv, n)
                return c[0]

            vector.wait_ge(lc, 32)
            for it in range(NT_A):
                xt = x_t[:, (it % 2) * D:(it % 2 + 1) * D]
                sqt = sq_t[:, (it % 2) * D:(it % 2 + 1) * D]
                xnt = xn_t[:, (it % 2) * D:(it % 2 + 1) * D]
                lx.wait(vector, it % 2, ctr.get(("a_ldx", it)))
                vector.reduce_sum(st_c[:, 0:1], xt, AX.X)
                vector.wait_ge(ca, ctr.get(("a_sq", it)))
                vector.reduce_sum(st_c[:, 1:2], sqt, AX.X)
                vector.drain()
                vector.tensor_scalar_mul(st_c[:, 2:3], st_c[:, 0:1], 1.0 / D)
                vector.drain()
                vector.tensor_mul(st_c[:, 3:4], st_c[:, 2:3], st_c[:, 2:3])
                vector.drain()
                vector.scalar_tensor_tensor(st_c[:, 4:5], st_c[:, 1:2], 1.0 / D,
                                            st_c[:, 3:4], ALU.mult, ALU.subtract)
                ctr.mark(("a_stats", it), inc(vector.drain()))
                vector.wait_ge(ca, ctr.get(("a_rstd", it)))
                if it >= 2:
                    vector.wait_ge(ct, ctr.get(("a_tp", it - 2, KD - 1)))
                ctr.mark(("a_xn", it),
                         inc(vector.tensor_scalar(
                             xnt, xt, st_c[:, 2:3], st_c[:, 5:6],
                             ALU.subtract, ALU.mult)))
                for k in range(KD):
                    vector.wait_ge(ct, ctr.get(("a_tp", it, k)))
                    ctr.mark(("a_tpe", it, k),
                             inc(vector.tensor_copy(
                                 xnT[:, k * TA + it * 128:
                                     k * TA + (it + 1) * 128],
                                 tp_ps[:, (k % 2) * 128:(k % 2 + 1) * 128])))
            for k in range(KD):
                lwx.wait(vector, k % 2, ctr.get(("a_ldw", k)))
                ctr.mark(("a_wxc", k),
                         inc(vector.tensor_copy(
                             wx_b[:, k * E:(k + 1) * E],
                             wx_st[:, (k % 2) * E:(k % 2 + 1) * E])))

        @block.scalar
        def _(scalar):
            c = [0]

            def inc(instr, n=1):
                c[0] += n
                instr.then_inc(ca, n)
                return c[0]

            scalar.wait_ge(lc, 32)
            for it in range(NT_A):
                xt = x_t[:, (it % 2) * D:(it % 2 + 1) * D]
                sqt = sq_t[:, (it % 2) * D:(it % 2 + 1) * D]
                lx.wait(scalar, it % 2, ctr.get(("a_ldx", it)))
                if it >= 2:
                    scalar.wait_ge(cv, ctr.get(("a_stats", it - 2)))
                ctr.mark(("a_sq", it), inc(scalar.activation(sqt, xt, ACT.Square)))
                scalar.wait_ge(cv, ctr.get(("a_stats", it)))
                scalar.activation(st_c[:, 5:6], st_c[:, 4:5], ACT.Ln,
                                  bias=cst[:, EPS1_I:EPS1_I + 1])
                scalar.drain()
                ctr.mark(("a_rstd", it),
                         inc(scalar.activation(st_c[:, 5:6], st_c[:, 5:6],
                                               ACT.Exp, scale=-0.5)))
            for p in range(34):
                eti, th = divmod(p, 2)
                scalar.wait_ge(ct, ctr.get(("a_mm", p)))
                if p >= 2:
                    soev.wait(scalar, (p - 2) % 4, ctr.get(("a_so_ev", p - 2)))
                if p >= 4:
                    soev.wait(scalar, p % 4, ctr.get(("a_so_ev", p - 4)))
                pse = mm_ps[:, (p % 2) * 512:(p % 2 + 1) * 512]
                buf = ev_t[:, (p % 4) * 512:(p % 4 + 1) * 512]
                sp = sp_t[:, (p % 2) * 512:(p % 2 + 1) * 512]
                if eti < KD:
                    scalar.activation(sp, pse, ACT.Abs)
                    scalar.drain()
                    scalar.activation(sp, sp, ACT.Exp, scale=-1.0)
                    scalar.drain()
                    scalar.activation(sp, sp, ACT.Ln, bias=1.0)
                    scalar.drain()
                    scalar.activation(buf, pse, ACT.Relu)
                    ctr.mark(("a_ev", p), inc(scalar.drain()))
                else:
                    ctr.mark(("a_ev", p),
                             inc(scalar.copy(
                                 bc_bf[:, (p % 2) * 512:(p % 2 + 1) * 512],
                                 pse[0:32, :])))
            for i, (src_t, row, c0, wid) in enumerate(WSL):
                w = w_sl[:, (i % 2) * (H // 2):(i % 2) * (H // 2) + wid]
                s = s_sl[:, (i % 2) * (H // 2):(i % 2) * (H // 2) + wid]
                lw.wait(scalar, i % 2, ctr.get(("a_ldwsl", i)))
                if i >= 2:
                    sosg.wait(scalar, i % 2, ctr.get(("a_so_sg", i - 2)))
                ctr.mark(("a_sign", i), inc(scalar.activation(s, w, ACT.Sign)))

        @block.tensor
        def _(tensor):
            c = [0]

            def inc(instr, n=1):
                c[0] += n
                instr.then_inc(ct, n)
                return c[0]

            tensor.wait_ge(lc, 32)
            for it in range(NT_A):
                xnt = xn_t[:, (it % 2) * D:(it % 2 + 1) * D]
                tensor.wait_ge(cv, ctr.get(("a_xn", it)))
                for k in range(KD):
                    if it * KD + k >= 2:
                        pit, pk = divmod(it * KD + k - 2, KD)
                        tensor.wait_ge(cv, ctr.get(("a_tpe", pit, pk)))
                    ctr.mark(("a_tp", it, k),
                             inc(tensor.transpose(
                                 tp_ps[:, (k % 2) * 128:(k % 2 + 1) * 128],
                                 xnt[:, k * 128:(k + 1) * 128], eye[:])))
            tensor.wait_ge(cv, ctr.get(("a_wxc", KD - 1)))
            for p in range(34):
                eti, th = divmod(p, 2)
                nparts = 128 if eti < 16 else 32
                pse = mm_ps[:, (p % 2) * 512:(p % 2 + 1) * 512]
                if p >= 2:
                    tensor.wait_ge(ca, ctr.get(("a_ev", p - 2)))
                if p == 0:
                    tensor.wait_ge(cv, ctr.get(("a_tpe", 3, KD - 1)))
                if p == 1:
                    tensor.wait_ge(cv, ctr.get(("a_tpe", NT_A - 1, KD - 1)))
                last = None
                for k in range(KD):
                    last = tensor.matmul(
                        pse[0:nparts, :],
                        wx_b[:, k * E + eti * 128:k * E + eti * 128 + nparts],
                        xnT[:, k * TA + th * 512:k * TA + (th + 1) * 512],
                        start=(k == 0), stop=(k == KD - 1))
                ctr.mark(("a_mm", p), inc(last))

        @block.gpsimd
        def _(gp):
            for p in range(34):
                eti, th = divmod(p, 2)
                buf = ev_t[:, (p % 4) * 512:(p % 4 + 1) * 512]
                sp = sp_t[:, (p % 2) * 512:(p % 2 + 1) * 512]
                gp.wait_ge(ca, ctr.get(("a_ev", p)))
                if eti < KD:
                    gp.tensor_add(buf, buf, sp)
                    ctr.mark(("a_so_ev", p),
                             soev.issue(gp, p % 4,
                                        dt_o[eti * 128:(eti + 1) * 128,
                                             th * 512:(th + 1) * 512], buf))
                else:
                    ctr.mark(("a_so_ev", p),
                             soev.issue(gp, p % 4,
                                        bc_o[:, th * 512:(th + 1) * 512],
                                        bc_bf[:, (p % 2) * 512:(p % 2 + 1) * 512]))
            cx = [0]
            gp.wait_ge(cv, ctr.get(("a_tpe", NT_A - 1, KD - 1)))
            for k in range(KD):
                gp.dma_start(out=xnT_o[k * 128:(k + 1) * 128, :],
                             in_=xnT[:, k * TA:(k + 1) * TA]).then_inc(sox, 16)
                cx[0] += 16
            for i, (src_t, row, c0, wid) in enumerate(WSL):
                s = s_sl[:, (i % 2) * (H // 2):(i % 2) * (H // 2) + wid]
                gp.wait_ge(ca, ctr.get(("a_sign", i)))
                if i < 4:
                    dst = sg_o[row * 128:(row + 1) * 128, c0:c0 + wid]
                elif i < 8:
                    dst = su_o[row * 128:(row + 1) * 128, c0:c0 + wid]
                else:
                    dst = sd_o[row * 128:(row + 1) * 128, :]
                ctr.mark(("a_so_sg", i), sosg.issue(gp, i % 2, dst, s))
            gp.wait_ge(sox, cx[0])
            for sl in range(4):
                gp.wait_ge(soev.sems[sl], soev.count[sl])
            for sl in range(2):
                gp.wait_ge(sosg.sems[sl], sosg.count[sl])
    return nc




def build_phase_b(ctr):
    nc = bass.Bass("TRN2", target_bir_lowering=False)
    dt_h = nc.dram_tensor("dt_sl", [BSZ, DS, L], F32, kind="ExternalInput").ap()
    xn_h = nc.dram_tensor("xn_sl", [BSZ, DS, L], BF16, kind="ExternalInput").ap()
    bc_h = nc.dram_tensor("bc_all", [32, BSZ * L], BF16, kind="ExternalInput").ap()
    al_h = nc.dram_tensor("alog_sl", [DS, N], F32, kind="ExternalInput").ap()
    dv_h = nc.dram_tensor("dv_sl", [DS, 1], F32, kind="ExternalInput").ap()

    y_o = nc.dram_tensor("y_o", [BSZ, DS, L], F32, kind="ExternalOutput").ap()

    NU = BSZ * NTB

    with contextlib.ExitStack() as es:
        cst_h, cst = _consts_setup(nc, es)
        a_sl = _mk(nc, es, "a_sl", [128, 2 * N])
        alr = _mk(nc, es, "alr", [128, 2 * N])
        dv = _mk(nc, es, "dv", [128, 2])
        dt_t = _mk(nc, es, "dt_t", [128, 2 * 2 * TB])
        xn_t2 = _mk(nc, es, "xn_t2", [128, 2 * 2 * TB], BF16)
        brep = _mk(nc, es, "brep", [128, N * TB], BF16)
        crep = _mk(nc, es, "crep", [128, N * TB], BF16)
        z_t = _mk(nc, es, "z_t", [128, TB])
        abar = _mk(nc, es, "abar", [128, N * TB])
        beta = _mk(nc, es, "beta", [128, N * TB])
        h_t = _mk(nc, es, "h_t", [128, N * TB], BF16)
        hc_t = _mk(nc, es, "hc_t", [128, N * TB], BF16)
        carry = _mk(nc, es, "carry", [128, BSZ * 2 * N], BF16)
        y_t = _mk(nc, es, "y_t", [128, 4 * TB])

        lc = es.enter_context(nc.semaphore("lc"))
        ldt = DmaSlots(nc, es, "ldt", 2)
        lb = es.enter_context(nc.semaphore("lb"))
        cv = es.enter_context(nc.semaphore("cv"))
        ca = es.enter_context(nc.semaphore("ca"))
        cp = es.enter_context(nc.semaphore("cp"))
        soy = DmaSlots(nc, es, "soy", 2)
        block = es.enter_context(nc.Block())

        def unit(u):
            return divmod(u, NTB)

        @block.sync
        def _(sync):
            for dst, srch in [(cst[:], cst_h[:]),
                              (alr[0:128, 0:N], al_h[0:128, :]),
                              (alr[0:128, N:2 * N], al_h[128:256, :]),
                              (dv[0:128, 0:1], dv_h[0:128, :]),
                              (dv[0:128, 1:2], dv_h[128:256, :])]:
                sync.dma_start(out=dst, in_=srch).then_inc(lc, 16)
            ctr.mark("b_ld0", 80)
            cb = [0]
            for u in range(BSZ * NTB):
                b, tt = unit(u)
                t0 = tt * TB
                if u >= 2:
                    sync.wait_ge(cv, ctr.get(("b_y", u - 2, 1)))
                for dti in range(2):
                    ldt.issue(sync, u % 2,
                              dt_t[:, ((u % 2) * 2 + dti) * TB:
                                   ((u % 2) * 2 + dti + 1) * TB],
                              dt_h[b, dti * 128:(dti + 1) * 128, t0:t0 + TB])
                    ctr.mark(("b_ldu", u),
                             ldt.issue(sync, u % 2,
                                       xn_t2[:, ((u % 2) * 2 + dti) * TB:
                                             ((u % 2) * 2 + dti + 1) * TB],
                                       xn_h[b, dti * 128:(dti + 1) * 128,
                                            t0:t0 + TB]))
                tg = b * L + tt * TB
                if u >= 1:
                    sync.wait_ge(cp, ctr.get(("b_beta", u - 1, 1)))
                    sync.wait_ge(cv, ctr.get(("b_hc", u - 1, 1)))
                for n in range(N):
                    sync.dma_start(
                        out=brep[:, n * TB:(n + 1) * TB],
                        in_=bc_h[n:n + 1, tg:tg + TB].broadcast_to([128, TB])
                    ).then_inc(lb, 16)
                    cb[0] += 16
                    sync.dma_start(
                        out=crep[:, n * TB:(n + 1) * TB],
                        in_=bc_h[N + n:N + n + 1, tg:tg + TB].broadcast_to([128, TB])
                    ).then_inc(lb, 16)
                    cb[0] += 16
                ctr.mark(("b_rep", u), cb[0])

        @block.scalar
        def _(scalar):
            c = [0]

            def inc(instr, n=1):
                c[0] += n
                instr.then_inc(ca, n)
                return c[0]

            scalar.wait_ge(lc, ctr.get("b_ld0"))
            ctr.mark("b_aexp", inc(scalar.activation(a_sl[:], alr[:], ACT.Exp)))
            for u in range(BSZ * NTB):
                for dti in range(2):
                    scalar.wait_ge(cv, ctr.get(("b_arg", u, dti)))
                    if (u, dti) > (0, 0):
                        pu, pd = (u, 0) if dti == 1 else (u - 1, 1)
                        scalar.wait_ge(cv, ctr.get(("b_scan", pu, pd)))
                    ctr.mark(("b_abar", u, dti),
                             inc(scalar.activation(abar[:], abar[:], ACT.Exp)))

        @block.vector
        def _(vector):
            c = [0]

            def inc(instr, n=1):
                c[0] += n
                instr.then_inc(cv, n)
                return c[0]

            vector.wait_ge(lc, ctr.get("b_ld0"))
            vector.wait_ge(ca, ctr.get("b_aexp"))
            for u in range(BSZ * NTB):
                b, tt = unit(u)
                for dti in range(2):
                    dtt = dt_t[:, ((u % 2) * 2 + dti) * TB:
                               ((u % 2) * 2 + dti + 1) * TB]
                    xnt = xn_t2[:, ((u % 2) * 2 + dti) * TB:
                                ((u % 2) * 2 + dti + 1) * TB]
                    ldt.wait(vector, u % 2, ctr.get(("b_ldu", u)))
                    if (u, dti) > (0, 0):
                        pu, pd = (u, 0) if dti == 1 else (u - 1, 1)
                        vector.wait_ge(cp, ctr.get(("b_beta", pu, pd)))
                    ctr.mark(("b_z", u, dti), inc(vector.tensor_mul(
                        z_t[:], dtt, xnt)))
                    if (u, dti) > (0, 0):
                        pu, pd = (u, 0) if dti == 1 else (u - 1, 1)
                        vector.wait_ge(cv, ctr.get(("b_scan", pu, pd)))
                    dt_b = dtt.unsqueeze(1).broadcast_to([128, N, TB])
                    a_b = a_sl[:, dti * N:(dti + 1) * N].unsqueeze(2) \
                        .broadcast_to([128, N, TB])
                    ctr.mark(("b_arg", u, dti), inc(vector.scalar_tensor_tensor(
                        abar[:].rearrange("p (n t) -> p n t", n=N),
                        dt_b, -1.0, a_b, ALU.mult, ALU.mult)))
                    vector.wait_ge(ca, ctr.get(("b_abar", u, dti)))
                    vector.wait_ge(cp, ctr.get(("b_beta", u, dti)))
                    last = None
                    for n in range(N):
                        ini = 0.0 if tt == 0 else \
                            carry[:, ((b * 2 + dti) * N + n):
                                  ((b * 2 + dti) * N + n + 1)]
                        last = vector.tensor_tensor_scan(
                            h_t[:, n * TB:(n + 1) * TB],
                            abar[:, n * TB:(n + 1) * TB],
                            beta[:, n * TB:(n + 1) * TB],
                            ini, ALU.mult, ALU.add)
                    ctr.mark(("b_scan", u, dti), inc(last))
                    vector.drain()
                    h3 = h_t[:].rearrange("p (n t) -> p n t", n=N)
                    vector.tensor_copy(
                        carry[:, (b * 2 + dti) * N:(b * 2 + dti + 1) * N],
                        h3[:, :, TB - 1])
                    vector.wait_ge(lb, ctr.get(("b_rep", u)))
                    ctr.mark(("b_hc", u, dti),
                             inc(vector.tensor_mul(hc_t[:], h_t[:], crep[:])))
                    vector.drain()
                    w = N
                    while w > 1:
                        w //= 2
                        vector.tensor_add(hc_t[:, 0:w * TB], hc_t[:, 0:w * TB],
                                          hc_t[:, w * TB:2 * w * TB])
                        vector.drain()
                    if u >= 2:
                        soy.wait(vector, u % 2, ctr.get(("b_so", u - 2, 1)))
                    yslc = y_t[:, ((u % 2) * 2 + dti) * TB:
                               ((u % 2) * 2 + dti + 1) * TB]
                    ctr.mark(("b_y", u, dti), inc(vector.scalar_tensor_tensor(
                        yslc, xnt, dv[:, dti:dti + 1], hc_t[:, 0:TB],
                        ALU.mult, ALU.add)))

        @block.gpsimd
        def _(gp):
            c = [0]
            gp.wait_ge(lc, ctr.get("b_ld0"))

            def ydma(u):
                b, tt = unit(u)
                t0 = tt * TB
                for dti in range(2):
                    gp.wait_ge(cv, ctr.get(("b_y", u, dti)))
                    ctr.mark(("b_so", u, dti),
                             soy.issue(gp, u % 2,
                                       y_o[b, dti * 128:(dti + 1) * 128,
                                           t0:t0 + TB],
                                       y_t[:, ((u % 2) * 2 + dti) * TB:
                                           ((u % 2) * 2 + dti + 1) * TB]))

            for u in range(BSZ * NTB):
                b, tt = unit(u)
                for dti in range(2):
                    gp.wait_ge(cv, ctr.get(("b_z", u, dti)))
                    gp.wait_ge(lb, ctr.get(("b_rep", u)))
                    if (u, dti) > (0, 0):
                        pu, pd = (u, 0) if dti == 1 else (u - 1, 1)
                        gp.wait_ge(cv, ctr.get(("b_scan", pu, pd)))
                    z_b = z_t[:].unsqueeze(1).broadcast_to([128, N, TB])
                    instr = gp.tensor_mul(
                        beta[:].rearrange("p (n t) -> p n t", n=N),
                        z_b, brep[:].rearrange("p (n t) -> p n t", n=N))
                    instr.then_inc(cp, 1)
                    c[0] += 1
                    ctr.mark(("b_beta", u, dti), c[0])
                if u >= 1:
                    ydma(u - 1)
            ydma(BSZ * NTB - 1)
            for sl in range(2):
                gp.wait_ge(soy.sems[sl], soy.count[sl])
    return nc




HB = 256
NHB = H // HB
THALF = 512


def build_phase_c(ctr):
    nc = bass.Bass("TRN2", target_bir_lowering=False)
    x_h = nc.dram_tensor("xc", [TA, D], F32, kind="ExternalInput").ap()
    y_h = nc.dram_tensor("yc", [TA, D], F32, kind="ExternalInput").ap()
    sg_h = nc.dram_tensor("sg", [D, H], BF16, kind="ExternalInput").ap()
    su_h = nc.dram_tensor("su", [D, H], BF16, kind="ExternalInput").ap()
    sd_h = nc.dram_tensor("sd", [H, D], BF16, kind="ExternalInput").ap()
    eye_h = nc.dram_tensor("eye", [128, 128], BF16, kind="ExternalInput").ap()
    out_o = nc.dram_tensor("out_o", [TA, D], F32, kind="ExternalOutput").ap()
    sv_d = nc.dram_tensor("sv_d", [NT_A, 128], F32, kind="Internal").ap()

    def piece(J):
        th, r = divmod(J, 8)
        gen, tsub = divmod(r, 4)
        return th, gen, tsub

    with contextlib.ExitStack() as es:
        cst_h, cst = _consts_setup(nc, es)
        eye = _mk(nc, es, "eye_s", [128, 128], BF16)
        xld = _mk(nc, es, "xld", [128, D])
        yld = _mk(nc, es, "yld", [128, D])
        x1f = _mk(nc, es, "x1f", [128, D])
        sqb = _mk(nc, es, "sqb", [128, D], BF16)
        st_c = _mk(nc, es, "st_c", [128, 8])
        zf = _mk(nc, es, "zf", [128, D])
        aa = _mk(nc, es, "aa", [128, D])
        bb_t = _mk(nc, es, "bb_t", [128, D], BF16)
        sgn = _mk(nc, es, "sgn", [128, D], BF16)
        qb = _mk(nc, es, "qb", [128, D], BF16)
        sv_all = _mk(nc, es, "sv_all", [128, NT_A])
        sv_rep = _mk(nc, es, "sv_rep", [128, TA])
        zqT = _mk(nc, es, "zqT", [128, KD * TA], BF16)
        sgb = _mk(nc, es, "sgb", [128, 2 * KD * HB], BF16)
        sub = _mk(nc, es, "sub", [128, 2 * KD * HB], BF16)
        sdb = _mk(nc, es, "sdb", [128, 2 * 1024], BF16)
        t1 = _mk(nc, es, "t1", [128, THALF])
        g_t = _mk(nc, es, "g_t", [128, THALF], BF16)
        p_t = _mk(nc, es, "p_t", [128, KH * THALF], BF16)
        outf = _mk(nc, es, "outf", [128, 2 * 1024])
        psA = _mk(nc, es, "psA", [128, 2048], psum=True)
        psB = _mk(nc, es, "psB", [128, 2048], psum=True)

        lc = es.enter_context(nc.semaphore("lc"))
        lxy = DmaSlots(nc, es, "lxy", 1)
        lg = DmaSlots(nc, es, "lg", 2)
        lu = DmaSlots(nc, es, "lu", 2)
        lsd = DmaSlots(nc, es, "lsd", 2)
        lxy2 = DmaSlots(nc, es, "lxy2", 2)
        lsv = es.enter_context(nc.semaphore("lsv"))
        cv = es.enter_context(nc.semaphore("cv"))
        ca = es.enter_context(nc.semaphore("ca"))
        ct = es.enter_context(nc.semaphore("ct"))
        souts = DmaSlots(nc, es, "souts", 2)
        block = es.enter_context(nc.Block())

        tp_ev = [psA[:, 0:64].bitcast(BF16), psA[:, 64:128].bitcast(BF16)]
        ps_g = psA[:, 512:1024]
        ps_u = psA[:, 1024:1536]
        ps_d = [psA[:, 0:1024], psA[:, 1024:2048],
                psB[:, 0:1024], psB[:, 1024:2048]]

        @block.sync
        def _(sync):
            sync.dma_start(out=cst[:], in_=cst_h[:]).then_inc(lc, 16)
            sync.dma_start(out=eye[:], in_=eye_h[:]).then_inc(lc, 16)
            for it in range(NT_A):
                if it >= 1:
                    sync.wait_ge(ca, ctr.get(("c_tile_done_a", it - 1)))
                    sync.wait_ge(cv, ctr.get(("c_tile_done_v", it - 1)))
                lxy.issue(sync, 0, xld[:], x_h[it * 128:(it + 1) * 128, :])
                ctr.mark(("c_lxy", it),
                         lxy.issue(sync, 0, yld[:],
                                   y_h[it * 128:(it + 1) * 128, :]))
            for th in range(2):
                for hb in range(NHB):
                    j = th * NHB + hb
                    sl = j % 2
                    if j >= 2:
                        pth, phb = divmod(j - 2, NHB)
                        sync.wait_ge(ct, ctr.get(("c_guse", pth, phb)))
                    m = 0
                    for k in range(KD):
                        lg.issue(sync, sl,
                                 sgb[:, (sl * KD + k) * HB:(sl * KD + k + 1) * HB],
                                 sg_h[k * 128:(k + 1) * 128,
                                      hb * HB:(hb + 1) * HB])
                        m = lu.issue(sync, sl,
                                     sub[:, (sl * KD + k) * HB:
                                         (sl * KD + k + 1) * HB],
                                     su_h[k * 128:(k + 1) * 128,
                                          hb * HB:(hb + 1) * HB])
                    ctr.mark(("c_lw", th, hb), m)
                for gen in range(2):
                    for h in range(KH):
                        j2 = h % 2
                        if h >= 2:
                            sync.wait_ge(ct, ctr.get(("c_dmm", th, gen, h - 2)))
                        elif (th, gen) > (0, 0):
                            pth, pgen = (th, gen - 1) if gen else (th - 1, 1)
                            sync.wait_ge(ct, ctr.get(("c_dmm", pth, pgen,
                                                      KH - 2 + h)))
                        ctr.mark(("c_lsd", th, gen, h),
                                 lsd.issue(sync, j2,
                                           sdb[:, j2 * 1024:(j2 + 1) * 1024],
                                           sd_h[h * 128:(h + 1) * 128,
                                                gen * 1024:(gen + 1) * 1024]))

        @block.vector
        def _(vector):
            c = [0]

            def inc(instr, n=1):
                c[0] += n
                instr.then_inc(cv, n)
                return c[0]

            vector.wait_ge(lc, 32)
            for it in range(NT_A):
                lxy.wait(vector, 0, ctr.get(("c_lxy", it)))
                ctr.mark(("c_x1", it),
                         inc(vector.tensor_add(x1f[:], xld[:], yld[:])))
                vector.reduce_sum(st_c[:, 0:1], x1f[:], AX.X)
                vector.wait_ge(ca, ctr.get(("c_sq", it)))
                vector.reduce_sum(st_c[:, 1:2], sqb[:], AX.X)
                vector.drain()
                vector.tensor_scalar_mul(st_c[:, 2:3], st_c[:, 0:1], 1.0 / D)
                vector.drain()
                vector.tensor_mul(st_c[:, 3:4], st_c[:, 2:3], st_c[:, 2:3])
                vector.drain()
                vector.scalar_tensor_tensor(st_c[:, 4:5], st_c[:, 1:2], 1.0 / D,
                                            st_c[:, 3:4], ALU.mult, ALU.subtract)
                ctr.mark(("c_stats", it), inc(vector.drain()))
                vector.wait_ge(ca, ctr.get(("c_rstd", it)))
                vector.tensor_scalar(zf[:], x1f[:], st_c[:, 2:3], st_c[:, 5:6],
                                     ALU.subtract, ALU.mult)
                vector.drain()
                vector.tensor_reduce(st_c[:, 6:7], zf[:], AX.X, ALU.max,
                                     apply_absolute_value=True)
                vector.drain()
                vector.reciprocal(st_c[:, 7:8], st_c[:, 6:7])
                vector.drain()
                vector.tensor_scalar_mul(st_c[:, 7:8], st_c[:, 7:8], 127.0)
                vector.tensor_scalar_mul(sv_all[:, it:it + 1], st_c[:, 6:7],
                                         1.0 / 127.0)
                vector.drain()
                ctr.mark(("c_v", it), inc(vector.tensor_scalar_mul(
                    zf[:], zf[:], st_c[:, 7:8])))
                vector.wait_ge(ca, ctr.get(("c_abs", it)))
                vector.tensor_copy(zf[:], aa[:])
                vector.drain()
                for kv in LADDER:
                    vector.tensor_single_scalar(bb_t[:], zf[:], kv - 0.5,
                                                ALU.is_ge)
                    vector.drain()
                    vector.scalar_tensor_tensor(zf[:], bb_t[:], -kv, zf[:],
                                                ALU.mult, ALU.add)
                    vector.drain()
                vector.tensor_sub(zf[:], aa[:], zf[:])
                vector.drain()
                vector.tensor_mul(zf[:], zf[:], sgn[:])
                vector.drain()
                vector.tensor_scalar(qb[:], zf[:], -128.0, 127.0,
                                     ALU.max, ALU.min)
                ctr.mark(("c_q", it), inc(vector.drain()))
                m = 0
                for k in range(KD):
                    vector.wait_ge(ct, ctr.get(("c_tp", it, k)))
                    m = inc(vector.tensor_copy(
                        zqT[:, k * TA + it * 128:k * TA + (it + 1) * 128],
                        tp_ev[k % 2]))
                    ctr.mark(("c_tpe", it, k), m)
                ctr.mark(("c_tile_done_v", it), m)
            vector.wait_ge(lsv, 16 * NT_A + 16)
            ctr.mark("c_svrep", inc(vector.tensor_copy(
                sv_rep[:, 0:1], sv_rep[:, 0:1])))
            for th in range(2):
                if th == 1:
                    vector.wait_ge(ct, ctr.get(("c_dmm_last", 0, 1)))
                for hti in range(KH):
                    hb, ht = divmod(hti, HB // 128)
                    vector.wait_ge(ct, ctr.get(("c_gmm", th, hb, ht)))
                    vector.tensor_mul(t1[:], ps_g,
                                      sv_rep[:, th * THALF:(th + 1) * THALF])
                    ctr.mark(("c_t1", th, hti), inc(vector.drain()))
                    vector.wait_ge(ca, ctr.get(("c_silu", th, hti)))
                    vector.wait_ge(ct, ctr.get(("c_umm", th, hb, ht)))
                    ctr.mark(("c_p", th, hti), inc(vector.tensor_mul(
                        p_t[:, hti * THALF:(hti + 1) * THALF], g_t[:], ps_u)))
                for gen in range(2):
                    vector.wait_ge(ct, ctr.get(("c_dmm_last", th, gen)))
                    for tsub in range(4):
                        J = th * 8 + gen * 4 + tsub
                        lxy2.wait(vector, J % 2, ctr.get(("c_lxy2", J)))
                        if J >= 2:
                            souts.wait(vector, J % 2, ctr.get(("c_souts", J - 2)))
                        it2 = th * 4 + tsub
                        ofs = outf[:, (J % 2) * 1024:(J % 2 + 1) * 1024]
                        vector.scalar_tensor_tensor(
                            ofs, ps_d[tsub], sv_all[:, it2:it2 + 1],
                            xld[:, (J % 2) * 1024:(J % 2 + 1) * 1024],
                            ALU.mult, ALU.add)
                        vector.drain()
                        ctr.mark(("c_oev", J), inc(vector.tensor_add(
                            ofs, ofs,
                            yld[:, (J % 2) * 1024:(J % 2 + 1) * 1024])))

        @block.scalar
        def _(scalar):
            c = [0]

            def inc(instr, n=1):
                c[0] += n
                instr.then_inc(ca, n)
                return c[0]

            scalar.wait_ge(lc, 32)
            for it in range(NT_A):
                scalar.wait_ge(cv, ctr.get(("c_x1", it)))
                ctr.mark(("c_sq", it),
                         inc(scalar.activation(sqb[:], x1f[:], ACT.Square)))
                scalar.wait_ge(cv, ctr.get(("c_stats", it)))
                scalar.activation(st_c[:, 5:6], st_c[:, 4:5], ACT.Ln,
                                  bias=cst[:, EPS2_I:EPS2_I + 1])
                scalar.drain()
                ctr.mark(("c_rstd", it),
                         inc(scalar.activation(st_c[:, 5:6], st_c[:, 5:6],
                                               ACT.Exp, scale=-0.5)))
                scalar.wait_ge(cv, ctr.get(("c_v", it)))
                scalar.activation(aa[:], zf[:], ACT.Abs)
                scalar.drain()
                scalar.activation(sgn[:], zf[:], ACT.Sign)
                ctr.mark(("c_abs", it), inc(scalar.drain()))
                ctr.mark(("c_tile_done_a", it), c[0])
            for th in range(2):
                for hti in range(KH):
                    scalar.wait_ge(cv, ctr.get(("c_t1", th, hti)))
                    ctr.mark(("c_silu", th, hti),
                             inc(scalar.activation(g_t[:], t1[:], ACT.Silu)))

        @block.tensor
        def _(tensor):
            c = [0]

            def inc(instr, n=1):
                c[0] += n
                instr.then_inc(ct, n)
                return c[0]

            tensor.wait_ge(lc, 32)
            for it in range(NT_A):
                tensor.wait_ge(cv, ctr.get(("c_q", it)))
                for k in range(KD):
                    if it * KD + k >= 2:
                        pit, pk = divmod(it * KD + k - 2, KD)
                        tensor.wait_ge(cv, ctr.get(("c_tpe", pit, pk)))
                    ctr.mark(("c_tp", it, k),
                             inc(tensor.transpose(
                                 tp_ev[k % 2],
                                 qb[:, k * 128:(k + 1) * 128], eye[:])))
            for th in range(2):
                tensor.wait_ge(cv, ctr.get("c_svrep"))
                if th == 0:
                    tensor.wait_ge(cv, ctr.get(("c_tile_done_v", NT_A - 1)))
                else:
                    tensor.wait_ge(cv, ctr.get(("c_oev", 7)))
                for hb in range(NHB):
                    sl = (th * NHB + hb) % 2
                    for ht in range(HB // 128):
                        hti = hb * (HB // 128) + ht
                        lg.wait(tensor, sl, ctr.get(("c_lw", th, hb)))
                        lu.wait(tensor, sl, ctr.get(("c_lw", th, hb)))
                        if (th, hti) > (0, 0):
                            pth, phti = (th, hti - 1) if hti else (th - 1, KH - 1)
                            tensor.wait_ge(cv, ctr.get(("c_t1", pth, phti)))
                        last = None
                        for k in range(KD):
                            last = tensor.matmul(
                                ps_g,
                                sgb[:, (sl * KD + k) * HB + ht * 128:
                                    (sl * KD + k) * HB + ht * 128 + 128],
                                zqT[:, k * TA + th * THALF:
                                    k * TA + (th + 1) * THALF],
                                start=(k == 0), stop=(k == KD - 1))
                        ctr.mark(("c_gmm", th, hb, ht), inc(last))
                        if (th, hti) > (0, 0):
                            pth, phti = (th, hti - 1) if hti else (th - 1, KH - 1)
                            tensor.wait_ge(cv, ctr.get(("c_p", pth, phti)))
                        for k in range(KD):
                            last = tensor.matmul(
                                ps_u,
                                sub[:, (sl * KD + k) * HB + ht * 128:
                                    (sl * KD + k) * HB + ht * 128 + 128],
                                zqT[:, k * TA + th * THALF:
                                    k * TA + (th + 1) * THALF],
                                start=(k == 0), stop=(k == KD - 1))
                        ctr.mark(("c_umm", th, hb, ht), inc(last))
                    ctr.mark(("c_guse", th, hb), c[0])
                tensor.wait_ge(cv, ctr.get(("c_p", th, KH - 1)))
                for gen in range(2):
                    if gen == 1:
                        for ts in range(4):
                            tensor.wait_ge(cv, ctr.get(("c_oev", th * 8 + ts)))
                    for h in range(KH):
                        lsd.wait(tensor, h % 2, ctr.get(("c_lsd", th, gen, h)))
                        last = None
                        for tsub in range(4):
                            for dh in range(2):
                                last = tensor.matmul(
                                    ps_d[tsub][:, dh * 512:(dh + 1) * 512],
                                    p_t[:, h * THALF + tsub * 128:
                                        h * THALF + tsub * 128 + 128],
                                    sdb[:, (h % 2) * 1024 + dh * 512:
                                        (h % 2) * 1024 + (dh + 1) * 512],
                                    start=(h == 0), stop=(h == KH - 1))
                        ctr.mark(("c_dmm", th, gen, h), inc(last))
                    ctr.mark(("c_dmm_last", th, gen), c[0])

        @block.gpsimd
        def _(gp):
            gp.wait_ge(cv, ctr.get(("c_tile_done_v", NT_A - 1)))
            gp.wait_ge(ca, ctr.get(("c_tile_done_a", NT_A - 1)))
            for it3 in range(NT_A):
                gp.dma_start(out=sv_d[it3:it3 + 1, :], in_=sv_all[:, it3:it3 + 1],
                             allow_slow_non_contiguous=True).then_inc(lsv, 16)
            gp.wait_ge(lsv, 16 * NT_A)
            svv = sv_d[:, :].unsqueeze(0).broadcast_to([128, NT_A, 128])
            gp.dma_start(out=sv_rep[:].rearrange("p (i q) -> p i q", i=NT_A),
                         in_=svv).then_inc(lsv, 16)
            for J in range(16):
                th, gen, tsub = piece(J)
                it = th * 4 + tsub
                if J >= 2:
                    gp.wait_ge(cv, ctr.get(("c_oev", J - 2)))
                lxy2.issue(gp, J % 2,
                           xld[:, (J % 2) * 1024:(J % 2 + 1) * 1024],
                           x_h[it * 128:(it + 1) * 128,
                               gen * 1024:(gen + 1) * 1024])
                ctr.mark(("c_lxy2", J),
                         lxy2.issue(gp, J % 2,
                                    yld[:, (J % 2) * 1024:(J % 2 + 1) * 1024],
                                    y_h[it * 128:(it + 1) * 128,
                                        gen * 1024:(gen + 1) * 1024]))
                gp.wait_ge(cv, ctr.get(("c_oev", J)))
                ctr.mark(("c_souts", J),
                         souts.issue(gp, J % 2,
                                     out_o[it * 128:(it + 1) * 128,
                                           gen * 1024:(gen + 1) * 1024],
                                     outf[:, (J % 2) * 1024:(J % 2 + 1) * 1024]))
            for sl2 in range(2):
                gp.wait_ge(souts.sems[sl2], souts.count[sl2])
    return nc


def _build2(f):
    ctr = Ctr()
    f(ctr)
    return f(ctr)


def kernel(L0_attn_x, L0_attn_A_log, L0_attn_D, L0_attn_W_xproj,
           L0_ffn_W_g, L0_ffn_W_u, L0_ffn_W_d):
    x = np.ascontiguousarray(np.asarray(L0_attn_x, np.float32))
    A_log = np.ascontiguousarray(np.asarray(L0_attn_A_log, np.float32))
    Dv = np.ascontiguousarray(np.asarray(L0_attn_D, np.float32))
    wxT = np.ascontiguousarray(np.asarray(L0_attn_W_xproj, np.float32).T)
    WgT = np.ascontiguousarray(np.asarray(L0_ffn_W_g, np.float32).T)
    WuT = np.ascontiguousarray(np.asarray(L0_ffn_W_u, np.float32).T)
    WdT = np.ascontiguousarray(np.asarray(L0_ffn_W_d, np.float32).T)
    eye = np.eye(128, dtype=BB)
    cst = _host_inputs_consts()

    nc_a = _build2(build_phase_a)
    maps = []
    for c in range(NCORES):
        b, tq = divmod(c, 4)
        maps.append({
            "xa": np.ascontiguousarray(x[b, tq * TA:(tq + 1) * TA]),
            "wxT": wxT,
            "wg_sl": np.ascontiguousarray(WgT[DS * c:DS * (c + 1)]),
            "wu_sl": np.ascontiguousarray(WuT[DS * c:DS * (c + 1)]),
            "wd_sl": np.ascontiguousarray(WdT[HS * c:HS * (c + 1)]),
            "eye": eye, "cst": cst,
        })
    res = run_bass_kernel_spmd(nc_a, maps, core_ids=list(range(NCORES)))
    dt_all = np.empty((BSZ, D, L), np.float32)
    xn_all = np.empty((BSZ, D, L), BB)
    bc_all = np.empty((32, BSZ * L), BB)
    sg = np.empty((D, H), BB)
    su = np.empty((D, H), BB)
    sd = np.empty((H, D), BB)
    for c in range(NCORES):
        b, tq = divmod(c, 4)
        o = res.results[c]
        dt_all[b][:, tq * TA:(tq + 1) * TA] = o["dt_o"]
        xn_all[b][:, tq * TA:(tq + 1) * TA] = o["xnT_o"]
        bc_all[:, b * L + tq * TA:b * L + (tq + 1) * TA] = o["bc_o"]
        sg[DS * c:DS * (c + 1)] = o["sg_o"]
        su[DS * c:DS * (c + 1)] = o["su_o"]
        sd[HS * c:HS * (c + 1)] = o["sd_o"]

    nc_b = _build2(build_phase_b)
    maps = []
    for c in range(NCORES):
        sl = slice(DS * c, DS * (c + 1))
        maps.append({
            "dt_sl": np.ascontiguousarray(dt_all[:, sl]),
            "xn_sl": np.ascontiguousarray(xn_all[:, sl]),
            "bc_all": bc_all,
            "alog_sl": np.ascontiguousarray(A_log[sl]),
            "dv_sl": np.ascontiguousarray(Dv[sl][:, None]),
            "cst": cst,
        })
    res = run_bass_kernel_spmd(nc_b, maps, core_ids=list(range(NCORES)))
    y_all = np.empty((BSZ, D, L), np.float32)
    for c in range(NCORES):
        y_all[:, DS * c:DS * (c + 1)] = res.results[c]["y_o"]
    y_tm = np.ascontiguousarray(y_all.transpose(0, 2, 1))

    nc_c = _build2(build_phase_c)
    maps = []
    for c in range(NCORES):
        b, tq = divmod(c, 4)
        maps.append({
            "xc": np.ascontiguousarray(x[b, tq * TA:(tq + 1) * TA]),
            "yc": np.ascontiguousarray(y_tm[b, tq * TA:(tq + 1) * TA]),
            "sg": sg, "su": su, "sd": sd, "eye": eye, "cst": cst,
        })
    res = run_bass_kernel_spmd(nc_c, maps, core_ids=list(range(NCORES)))
    out = np.empty((BSZ, L, D), np.float32)
    for c in range(NCORES):
        b, tq = divmod(c, 4)
        out[b, tq * TA:(tq + 1) * TA] = res.results[c]["out_o"]
    return out
